# version 01 (final)
# baseline (speedup 1.0000x reference)
import sys
import numpy as np
import ml_dtypes
try:
    import concourse.bass as _probe_import
except ImportError:
    sys.path.insert(0, "/opt/trn_rl_repo")
import concourse.bass as bass
import concourse.mybir as mybir
from concourse.bass_utils import run_bass_kernel_spmd

F32 = mybir.dt.float32
BF16 = mybir.dt.bfloat16
I32 = mybir.dt.int32
U8 = mybir.dt.uint8
AF = mybir.ActivationFunctionType
ALU = mybir.AluOpType
AX = mybir.AxisListType
IOA = bass.IndirectOffsetOnAxis if hasattr(bass, "IndirectOffsetOnAxis") else None

E = 2048
S = 8192
NH = 16
HD = 128
FF = 8192
CORES = 8
CH = 512
T = 1024
NTT = 8
ENGS = ("sync", "tensor", "vector", "scalar", "gpsimd")
ITER_A, ITER_B = 8, 16
NIT = ITER_A + ITER_B

BF = ml_dtypes.bfloat16


def _hadamard(n):
    h = np.array([[1.0]], dtype=np.float64)
    while h.shape[0] < n:
        h = np.block([[h, h], [h, -h]])
    return h.astype(np.float32)


def _perm2048():
    idx = np.empty(E, dtype=np.int64)
    for p in range(16):
        for q in range(128):
            s, A = q // 16, q % 16
            idx[p * 128 + q] = 128 * A + 8 * p + s
    return idx


def _perm8192():
    idx = np.empty(FF, dtype=np.int64)
    for p in range(64):
        for q in range(128):
            s, A = q // 64, q % 64
            idx[p * 128 + q] = 128 * A + 2 * p + s
    return idx


_ALIAS_GROUPS = [
    ("trp", ["xqT_t", "aoT_t", "exr_t", "f2_xt"]),
    ("mmx", ["f2_hm", "f2_bm", "f8_bm", "smx_m", "wbc_mm", "cb_mm", "mp_mm",
             "qn_mm", "kn_mm", "f2_smm", "f8_smm", "gu_smm"]),
    ("dl", ["ms_st", "ms_ld", "kn_st", "kn_ld", "kn_ld2", "mp_ld", "f2_sl",
            "f8_sl", "gu_el", "idx_ld", "f2_pl", "f8_pl", "dn_rl"]),
    ("sv", ["cb", "cb2", "mp", "kn_g", "wf", "wmn", "wsc", "wbc", "vexv", "sov"]),
    ("sq", ["qn_sq", "kn_sq"]),
    ("nm", ["qn_m", "kn_m"]),
    ("guev", ["gu_gt", "gu_ut"]),
    ("qq", ["f2_q", "dn_q"]),
    ("cst8", ["f2_cast", "dn_cast"]),
    ("act1", ["expp", "gu_sil"]),
    ("wl", ["wsl", "wsl2"]),
    ("wls", ["qkv_wl", "o_wl", "gu_wl", "dn_wl"]),
    ("st1", ["kst_st", "vst_st"]),
    ("mxc", ["f2_mx", "f8_mx"]),
    ("mxcst", ["f2_mx_st", "f8_mx_st"]),
    ("ru", ["wsum_u", "tern_u"]),
]
for _t, _g in (("qwl0", []), ("qwl1", []), ("qwl2", []), ("qwl3", []),
               ("kg0", []), ("kg1", []), ("vg0", []), ("vg1", []),
               ("fst0", []), ("fst1", [])):
    pass
_ALIAS = {}
for tgt, names in _ALIAS_GROUPS:
    for n in names:
        _ALIAS[n] = tgt


class _ADict(dict):
    @staticmethod
    def _k(k):
        return _ALIAS.get(k, k)

    def get(self, k, d=None):
        return super().get(self._k(k), d)

    def __getitem__(self, k):
        return super().__getitem__(self._k(k))

    def __setitem__(self, k, v):
        super().__setitem__(self._k(k), v)

    def __contains__(self, k):
        return super().__contains__(self._k(k))


class Prog:
    def __init__(self):
        self.nc = bass.Bass("TRN2", target_bir_lowering=False)
        self.q = {e: [] for e in ENGS}
        self.count = _ADict()

    def tok(self, name):
        name = _ALIAS.get(name, name)
        return (name, self.count.get(name, 0))

    def emit(self, eng, fn, deps=(), inc=None, amt=1):
        deps2 = []
        for d in deps:
            if d is None:
                continue
            n, v = d
            n = _ALIAS.get(n, n)
            assert v <= self.count.get(n, 0), f"future dep {n}:{v} > {self.count.get(n,0)}"
            if v > 0:
                deps2.append((n, v))
        if inc is not None:
            inc = _ALIAS.get(inc, inc)
            self.count[inc] = self.count.get(inc, 0) + amt
        self.q[eng].append((tuple(deps2), fn, inc, amt))
        return (inc, self.count[inc]) if inc else None

    def dma(self, eng, out, in_, deps=(), inc=None, **kw):
        return self.emit(eng, lambda e: e.dma_start(out=out, in_=in_, **kw),
                         deps=deps, inc=inc, amt=16)

    def build(self, final_toks):
        nc = self.nc
        names = set(self.count.keys())
        for e in ENGS:
            for (deps, _, _, _) in self.q[e]:
                names |= {n for (n, _) in deps}
        names |= {n for (n, _) in final_toks}
        import contextlib
        stack = contextlib.ExitStack()
        sems = {n: stack.enter_context(nc.semaphore(f"s_{n}")) for n in sorted(names)}
        blk = stack.enter_context(nc.Block())
        waited = {e: {} for e in ENGS}

        def runq(obj, name):
            seen = waited[name]
            for (deps, fn, inc, amt) in self.q[name]:
                for (n, v) in deps:
                    if seen.get(n, 0) < v:
                        obj.wait_ge(sems[n], v)
                        seen[n] = v
                r = fn(obj)
                if inc is not None:
                    r.then_inc(sems[inc], amt)

        @blk.sync
        def _(e):
            runq(e, "sync")
            for (n, v) in final_toks:
                if waited["sync"].get(n, 0) < v and v > 0:
                    e.wait_ge(sems[n], v)
                    waited["sync"][n] = v

        @blk.tensor
        def _(e):
            runq(e, "tensor")

        @blk.vector
        def _(e):
            runq(e, "vector")

        @blk.scalar
        def _(e):
            runq(e, "scalar")

        @blk.gpsimd
        def _(e):
            runq(e, "gpsimd")

        stack.close()
        return nc


def build_program(debug=0):
    p = Prog()
    nc = p.nc
    dbg_list = []

    def dbgout(name, shape, dt=F32):
        ap = nc.dram_tensor(name, list(shape), dt, kind="ExternalOutput").ap()
        dbg_list.append(name)
        return ap

    x_in = nc.dram_tensor("x", [T, E], F32, kind="ExternalInput").ap()
    WNM = ["wq", "wk", "wv", "wo", "wg", "wu", "wd"]
    WSHP = {"wq": (E // 8, E), "wk": (E // 8, E), "wv": (E // 8, E), "wo": (E // 8, E),
            "wg": (E // 8, FF), "wu": (E // 8, FF), "wd": (FF // 8, E)}
    WCOUNT = {"wq": E * E, "wk": E * E, "wv": E * E, "wo": E * E,
              "wg": E * FF, "wu": E * FF, "wd": FF * E}
    wsl = {nm: nc.dram_tensor(f"{nm}_sl", list(WSHP[nm]), F32, kind="ExternalInput").ap()
           for nm in WNM}
    h128_in = nc.dram_tensor("h128", [128, 128], F32, kind="ExternalInput").ap()
    bd16_in = nc.dram_tensor("bd16", [128, 128], F32, kind="ExternalInput").ap()
    bd64_in = nc.dram_tensor("bd64", [128, 128], F32, kind="ExternalInput").ap()
    idf_in = nc.dram_tensor("idf", [128, 128], F32, kind="ExternalInput").ap()
    idb_in = nc.dram_tensor("idb", [128, 128], BF16, kind="ExternalInput").ap()
    onesr_in = nc.dram_tensor("onesr", [1, 128], F32, kind="ExternalInput").ap()
    onescb_in = nc.dram_tensor("onescb", [128, 1], BF16, kind="ExternalInput").ap()
    mask_in = nc.dram_tensor("maskc", [128, 2048], BF16, kind="ExternalInput").ap()
    mpat_in = nc.dram_tensor("mpat", [1, NIT], F32, kind="ExternalInput").ap()
    idxk_in = nc.dram_tensor("idxk", [128, NIT], I32, kind="ExternalInput").ap()
    idxv_in = nc.dram_tensor("idxv", [128, NIT * 4], I32, kind="ExternalInput").ap()

    out = nc.dram_tensor("out", [T, E], F32, kind="ExternalOutput").ap()

    tern_in = {nm: nc.dram_tensor(f"ti_{nm}", list(WSHP[nm]), BF16).ap() for nm in WNM}
    tern_out = {nm: nc.dram_tensor(f"to_{nm}", [WSHP[nm][0] * 8, WSHP[nm][1]], BF16).ap()
                for nm in WNM}
    msum_in = nc.dram_tensor("msum_in", [8, 1], F32).ap()
    msum_out = nc.dram_tensor("msum_out", [8, 1], F32).ap()
    kt_loc = nc.dram_tensor("kt_loc", [4096, CH], BF16).ap()
    v_loc = nc.dram_tensor("v_loc", [T, E], BF16).ap()
    kt_g = nc.dram_tensor("kt_g", [CORES * 4096, CH], BF16).ap()
    v_g = nc.dram_tensor("v_g", [CORES * T, E], BF16).ap()
    kn_in = nc.dram_tensor("kn_in", [1, NH], F32).ap()
    kn_out = nc.dram_tensor("kn_out", [CORES, NH], F32).ap()
    r1a = nc.dram_tensor("r1a", [E, T], F32).ap()
    r1b = nc.dram_tensor("r1b", [FF, T], F32).ap()
    r2 = nc.dram_tensor("r2", [FF, T], F32).ap()
    srow2 = nc.dram_tensor("srow2", [2, T], F32).ap()
    srow8 = nc.dram_tensor("srow8", [2, T], F32).ap()

    AR = nc.alloc_sbuf_tensor("arena", [128, 204 * 1024], U8).ap()

    def vf32(kb_off, kb_len):
        return AR[:, int(kb_off * 1024):int((kb_off + kb_len) * 1024)].bitcast(F32)

    def vbf(kb_off, kb_len):
        return AR[:, int(kb_off * 1024):int((kb_off + kb_len) * 1024)].bitcast(BF16)

    def vi32(kb_off, kb_len):
        return AR[:, int(kb_off * 1024):int((kb_off + kb_len) * 1024)].bitcast(I32)

    X = vf32(0, 64)
    Bf = vf32(64, 64)
    Bb = vbf(64, 64)
    Ib = vbf(128, 32)
    If32 = vf32(128, 32)
    Ii32 = vi32(128, 32)
    Qb = vbf(160, 32)
    Qf = vf32(160, 32)
    Qi = vi32(160, 32)

    M0 = 192.0
    maskc = vbf(M0, 4)
    h128 = vf32(M0 + 4, 0.5)
    bd16 = vf32(M0 + 4.5, 0.5)
    bd64 = vf32(M0 + 5, 0.5)
    idf = vf32(M0 + 5.5, 0.5)
    idb = vbf(M0 + 6, 0.25)
    onesr = vf32(M0 + 6.25, 0.5)[0:1, 0:128]
    onescb = vbf(M0 + 6.75, 0.25)[:, 0:1]
    CV = vf32(M0 + 7, 1.0)
    CB = vf32(M0 + 8, 2.0)

    PS = nc.alloc_psum_tensor("ps", [128, 4096], F32).ap()
    PSB = PS.bitcast(BF16)

    def bank(i):
        return PS[:, i * 512:(i + 1) * 512]

    def bank2(i):
        return PS[:, i * 512:(i + 2) * 512]

    p.dma("sync", maskc, mask_in, inc="cst")
    p.dma("sync", h128, h128_in, inc="cst")
    p.dma("sync", bd16, bd16_in, inc="cst")
    p.dma("sync", bd64, bd64_in, inc="cst")
    p.dma("sync", idf, idf_in, inc="cst")
    p.dma("sync", idb, idb_in, inc="cst")
    p.dma("sync", onesr, onesr_in, inc="cst")
    p.dma("sync", onescb, onescb_in, inc="cst")
    CST = p.tok("cst")
    p.emit("vector", lambda e: e.memset(CV[:, 0:1], -0.5), inc="cv")
    p.emit("vector", lambda e: e.memset(CV[:, 1:2], 0.5), inc="cv")
    CVT = p.tok("cv")


    for tt in range(NTT):
        p.dma("sync", X[:, tt * E:(tt + 1) * E], x_in[tt * 128:(tt + 1) * 128, :], inc="x_ld")

    i32scr = Bf.bitcast(I32)
    for tt in range(NTT):
        xt = X[:, tt * E:(tt + 1) * E]
        s127 = CV[:, 8 + tt:9 + tt]
        ex = CV[:, 16 + tt:17 + tt]
        mx = CV[:, 64:65]
        t_ld = ("x_ld", 16 * NTT)
        p.emit("vector", lambda e, xt=xt, mx=mx: e.reduce_max(mx, xt, axis=AX.X, apply_absolute_value=True),
               deps=[t_ld, p.tok("xq_s")], inc="xq_s")
        p.emit("vector", lambda e, mx=mx: e.tensor_scalar_max(mx, mx, 1e-5), deps=[p.tok("xq_s")], inc="xq_s")
        p.emit("vector", lambda e, mx=mx, s127=s127: e.reciprocal(s127, mx), deps=[p.tok("xq_s")], inc="xq_s")
        p.emit("vector", lambda e, mx=mx, ex=ex: e.tensor_scalar_mul(ex, mx, 1.0 / 127.0),
               deps=[p.tok("xq_s")], inc="xq_s")
        p.emit("vector", lambda e, s127=s127: e.tensor_scalar_mul(s127, s127, 127.0),
               deps=[p.tok("xq_s")], inc="xq_s")
        scr = i32scr[:, (tt % 2) * E:(tt % 2 + 1) * E]
        p.emit("vector", lambda e, xt=xt, scr=scr, s127=s127: e.tensor_scalar(scr, xt, s127, None, ALU.mult),
               deps=[p.tok("xq_s"), ("xq_bf", max(0, p.count.get("xq_bf", 0) - 1))], inc="xq_i")
        xqb = Qb[:, tt * E:(tt + 1) * E]
        p.emit("scalar", lambda e, scr=scr, xqb=xqb: e.activation(xqb, scr, AF.Copy),
               deps=[p.tok("xq_i")], inc="xq_bf")
    XQBF = p.tok("xq_bf")

    for et in range(16):
        for tt in range(NTT):
            src = Qb[:, tt * E + et * 128: tt * E + (et + 1) * 128]
            pb = PSB[:, (6 + (p.count.get("xqT_t", 0) % 2)) * 1024:][:, 0:128]
            dst = Ib[:, et * 1024 + tt * 128: et * 1024 + (tt + 1) * 128]
            p.emit("tensor", lambda e, src=src, pb=pb: e.transpose(pb, src, idb),
                   deps=[("xq_bf", tt + 1 if et == 0 else NTT), CST,
                         ("xqT_e", max(0, p.count.get("xqT_e", 0) - 1))], inc="xqT_t")
            p.emit("vector", lambda e, pb=pb, dst=dst: e.tensor_copy(dst, pb),
                   deps=[p.tok("xqT_t")], inc="xqT_e")
    XQT = p.tok("xqT_e")

    if debug == 1:
        d1 = dbgout("d_xqt", [128, 16384], BF16)
        p.dma("sync", d1, Ib, deps=[XQT], inc="dbg")
        d2 = dbgout("d_ex", [128, 16], F32)
        p.dma("sync", d2, CV[:, 8:24], deps=[XQT], inc="dbg")
        return p.build([p.tok("dbg")]), dbg_list

    Bslot = [Bf[:, 0:8192], Bf[:, 8192:16384]]
    for wi, nm in enumerate(WNM):
        R, C = WSHP[nm]
        acc = CV[:, 72 + wi:73 + wi]
        p.emit("vector", lambda e, acc=acc: e.memset(acc, 0.0), deps=[XQBF], inc="wsum")
        for ti in range(R // 128):
            slot = Bslot[(wi + ti) % 2]
            sl = slot[:, 0:C]
            p.dma("sync", sl, wsl[nm][ti * 128:(ti + 1) * 128, :],
                  deps=[XQBF, ("wsum_u", p.count.get("wsum_u", 0))], inc="wsl")
            nblk = C // 512
            blk = CV[:, 96:96 + nblk]
            p.emit("vector", lambda e, sl=sl, blk=blk, nblk=nblk:
                   e.reduce_sum(blk, sl.rearrange("p (a b) -> p a b", a=nblk), axis=AX.X,
                                apply_absolute_value=True),
                   deps=[p.tok("wsl"), p.tok("wsum")], inc="wsum")
            t1 = CV[:, 112:113]
            p.emit("vector", lambda e, blk=blk, t1=t1: e.reduce_sum(t1, blk, axis=AX.X),
                   deps=[p.tok("wsum")], inc="wsum")
            p.emit("vector", lambda e, acc=acc, t1=t1: e.tensor_add(acc, acc, t1),
                   deps=[p.tok("wsum")], inc="wsum")
            p.emit("vector", lambda e: e.engine_nop(), deps=[p.tok("wsum")], inc="wsum_u")
    p.emit("vector", lambda e: e.tensor_copy(CV[0:64, 80:87], CV[64:128, 72:79]),
           deps=[p.tok("wsum")], inc="wf")
    p.emit("vector", lambda e: e.tensor_add(CV[0:64, 80:87], CV[0:64, 72:79], CV[0:64, 80:87]),
           deps=[p.tok("wf")], inc="wf")
    p.emit("vector", lambda e: e.tensor_copy(CV[0:32, 88:95], CV[32:64, 80:87]),
           deps=[p.tok("wf")], inc="wf")
    p.emit("vector", lambda e: e.tensor_add(CV[0:32, 88:95], CV[0:32, 80:87], CV[0:32, 88:95]),
           deps=[p.tok("wf")], inc="wf")
    p.emit("vector", lambda e: e.memset(CV[0:32, 96:128], 0.0), deps=[p.tok("wf")], inc="wf")
    p.emit("vector", lambda e: e.tensor_copy(CV[0:32, 96:103], CV[0:32, 88:95]),
           deps=[p.tok("wf")], inc="wf")
    p.emit("vector", lambda e: e.transpose(CV[0:32, 128:160], CV[0:32, 96:128]),
           deps=[p.tok("wf")], inc="wf")
    p.emit("vector", lambda e: e.reduce_sum(CV[0:8, 160:161], CV[0:8, 128:160], axis=AX.X),
           deps=[p.tok("wf")], inc="wf")
    p.dma("sync", msum_in, CV[0:8, 160:161], deps=[p.tok("wf")], inc="ms_st")
    p.emit("gpsimd", lambda e: e.collective_compute(
        "AllReduce", ALU.add, replica_groups=[list(range(CORES))],
        ins=[msum_in.opt()], outs=[msum_out.opt()]),
        deps=[p.tok("ms_st")], inc="cc_ms")
    p.emit("vector", lambda e: e.memset(CV[0:1, 176:192], 0.0), deps=[p.tok("cc_ms")], inc="wmn")
    p.dma("sync", CV[0:1, 164:172], msum_out.rearrange("a b -> b a"), deps=[p.tok("cc_ms")], inc="ms_ld")
    for wi, nm in enumerate(WNM):
        mcol = CV[0:1, 176 + wi:177 + wi]
        p.emit("vector", lambda e, wi=wi, mcol=mcol, nm=nm:
               e.tensor_scalar_mul(mcol, CV[0:1, 164 + wi:165 + wi], 1.0 / WCOUNT[nm]),
               deps=[p.tok("ms_ld"), p.tok("wmn")], inc="wmn")
        p.emit("vector", lambda e, mcol=mcol: e.tensor_scalar_max(mcol, mcol, 1e-5),
               deps=[p.tok("wmn")], inc="wmn")
    p.emit("vector", lambda e: e.reciprocal(CV[0:1, 184:191], CV[0:1, 176:183]),
           deps=[p.tok("wmn")], inc="wmn")
    p.emit("tensor", lambda e: e.matmul(bank(7)[:, 0:16], onesr, CV[0:1, 176:192],
                                        start=True, stop=True),
           deps=[p.tok("wmn"), CST, XQT], inc="wbc_mm")
    p.emit("vector", lambda e: e.tensor_copy(CV[:, 24:40], bank(7)[:, 0:16]),
           deps=[p.tok("wbc_mm")], inc="wbc")
    WBC = p.tok("wbc")

    def mean_c(nm):
        return CV[:, 24 + WNM.index(nm):25 + WNM.index(nm)]

    def sw_c(nm):
        return CV[:, 32 + WNM.index(nm):33 + WNM.index(nm)]

    for wi, nm in enumerate(WNM):
        fac = 0.5 / np.sqrt(128.0) if nm == "wq" else 0.5
        p.emit("vector", lambda e, wi=wi, fac=fac:
               e.tensor_scalar_mul(CV[:, 40 + wi:41 + wi], CV[:, 24 + wi:25 + wi], float(fac)),
               deps=[WBC], inc="wsc")
    WSC = p.tok("wsc")

    sa_t = Qb[:, 0:8192]
    sb_t = Qb[:, 8192:16384]
    for wi, nm in enumerate(WNM):
        R, C = WSHP[nm]
        for ti in range(R // 128):
            slot = Bslot[(wi + ti) % 2]
            sl = slot[:, 0:C]
            p.dma("sync", sl, wsl[nm][ti * 128:(ti + 1) * 128, :],
                  deps=[WBC, XQT, ("tern_u", p.count.get("tern_u", 0))], inc="wsl2")
            sa = sa_t[:, 0:C]
            sb = sb_t[:, 0:C]
            p.emit("scalar", lambda e, sl=sl, sa=sa, nm=nm:
                   e.activation(sa, sl, AF.Sign, bias=CV[:, 0:1], scale=sw_c(nm)),
                   deps=[p.tok("wsl2"), CVT, ("tern_st", p.count.get("tern_st", 0))],
                   inc="tern_s")
            p.emit("scalar", lambda e, sl=sl, sb=sb, nm=nm:
                   e.activation(sb, sl, AF.Sign, bias=CV[:, 1:2], scale=sw_c(nm)),
                   deps=[p.tok("tern_s")], inc="tern_s")
            p.emit("vector", lambda e, sa=sa, sb=sb: e.tensor_add(sa, sa, sb),
                   deps=[p.tok("tern_s")], inc="tern_a")
            p.emit("vector", lambda e: e.engine_nop(), deps=[p.tok("tern_a")], inc="tern_u")
            p.dma("sync", tern_in[nm][ti * 128:(ti + 1) * 128, :], sa,
                  deps=[p.tok("tern_a")], inc="tern_st")
    TERNST = p.tok("tern_st")
    cc_toks = {}
    for nm in ("wq", "wk", "wv", "wo", "wg", "wu", "wd"):
        cc_toks[nm] = p.emit("gpsimd", lambda e, nm=nm: e.collective_compute(
            "AllGather", ALU.bypass, replica_groups=[list(range(CORES))],
            ins=[tern_in[nm].opt()], outs=[tern_out[nm].opt()]),
            deps=[TERNST], inc=f"cc_{nm}")
    CCAT = [cc_toks[nm] for nm in ("wq", "wk", "wv", "wo")]
    CCFFN = [cc_toks[nm] for nm in ("wg", "wu", "wd")]

    if debug == 2:
        d1 = dbgout("d_mean", [1, 16], F32)
        p.dma("sync", d1, CV[0:1, 176:192], deps=[WBC], inc="dbg")
        d2 = dbgout("d_ternq", [256, E], BF16)
        p.dma("sync", d2, tern_out["wq"][0:256, :], deps=CCAT, inc="dbg")
        return p.build([p.tok("dbg")]), dbg_list

    SMX = Bf[:, 12288:13312]
    exrow = Bf[0:1, 13312:14336]
    for tt in range(NTT):
        ex = CV[:, 16 + tt:17 + tt]
        pb = bank(6)
        p.emit("tensor", lambda e, ex=ex, pb=pb: e.transpose(pb[0:1, 0:128], ex, idf),
               deps=[p.tok("xq_s"), CST, ("exr_e", p.count.get("exr_e", 0)),
                     p.tok("tern_u")],
               inc="exr_t")
        p.emit("vector", lambda e, pb=pb, tt=tt: e.tensor_copy(exrow[0:1, tt * 128:(tt + 1) * 128],
                                                               pb[0:1, 0:128]),
               deps=[p.tok("exr_t")], inc="exr_e")
    for hh in range(2):
        pb = bank(4 + hh)
        p.emit("tensor", lambda e, hh=hh, pb=pb: e.matmul(
            pb, onesr, exrow[0:1, hh * 512:(hh + 1) * 512], start=True, stop=True),
            deps=[p.tok("exr_e"), CST, ("smx_e", p.count.get("smx_e", 0))], inc="smx_m")
        p.emit("vector", lambda e, hh=hh, pb=pb: e.tensor_copy(SMX[:, hh * 512:(hh + 1) * 512], pb),
               deps=[p.tok("smx_m")], inc="smx_e")
    SMXD = p.tok("smx_e")

    WQs = [Bb[:, 0:8192], Bb[:, 8192:16384]]
    Kst = Bb[:, 28672:29696]
    Vst = Bb[:, 29696:30720]

    def wslice_load(nm, oc, slot, deps):
        src = tern_out[nm].rearrange("(a p) c -> p a c", p=128)[:, :, oc * 512:(oc + 1) * 512]
        dst = WQs[slot].rearrange("p (a c) -> p a c", a=16)
        for q4 in range(4):
            p.dma("sync", dst[:, q4 * 4:(q4 + 1) * 4], src[:, q4 * 4:(q4 + 1) * 4],
                  deps=deps, inc=f"qwl{slot}")

    qkv_bank_tok = {i: ("none", 0) for i in range(6)}
    qkv_bank_tok[4] = SMXD
    qkv_bank_tok[5] = SMXD
    slot_free = [("none", 0), ("none", 0)]

    for pi, nm in enumerate(("wk", "wv", "wq")):
        for oc in range(4):
            slot = (pi * 4 + oc) % 2
            wslice_load(nm, oc, slot, CCAT + [SMXD, slot_free[slot]])
            wl_tok = p.tok(f"qwl{slot}")
            grp = 0
            if nm in ("wq", "wk"):
                for tc in range(2):
                    for ot in range(4):
                        h = oc * 4 + ot
                        bk = (pi * 4 + oc * 2 + tc * 4 + ot) % 6
                        pb = bank(bk)
                        for et in range(16):
                            lhsT = WQs[slot][:, et * 512 + ot * 128: et * 512 + (ot + 1) * 128]
                            rhs = Ib[:, et * 1024 + tc * 512: et * 1024 + (tc + 1) * 512]
                            p.emit("tensor", lambda e, pb=pb, lhsT=lhsT, rhs=rhs, et=et:
                                   e.matmul(pb, lhsT, rhs, start=(et == 0), stop=(et == 15)),
                                   deps=[wl_tok, XQT, qkv_bank_tok[bk]], inc="qkv_mm")
                        mm_tok = p.tok("qkv_mm")
                        if nm == "wq":
                            dst = Qb[:, h * 1024 + tc * 512: h * 1024 + (tc + 1) * 512]
                            tv = p.emit("vector", lambda e, dst=dst, pb=pb, h=h, tc=tc:
                                        e.scalar_tensor_tensor(dst, pb, CV[:, 40:41],
                                                               SMX[:, tc * 512:(tc + 1) * 512],
                                                               ALU.mult, ALU.mult),
                                        deps=[mm_tok, WSC, SMXD], inc="qkv_ev")
                            qkv_bank_tok[bk] = tv
                        else:
                            tv = p.emit("vector", lambda e, pb=pb, h=h, tc=tc:
                                        e.scalar_tensor_tensor(Kst[:, 0:512], pb, CV[:, 41:42],
                                                               SMX[:, tc * 512:(tc + 1) * 512],
                                                               ALU.mult, ALU.mult),
                                        deps=[mm_tok, WSC, SMXD,
                                              ("kst_st", p.count.get("kst_st", 0))],
                                        inc="qkv_ev")
                            qkv_bank_tok[bk] = tv
                            p.dma("sync", kt_loc[tc * 2048 + h * 128: tc * 2048 + (h + 1) * 128, :],
                                  Kst[:, 0:512], deps=[tv], inc="kst_st")
            else:
                for tt in range(NTT):
                    bk = (oc * 2 + tt) % 6
                    pb = bank(bk)
                    for et in range(16):
                        lhsT = Ib[:, et * 1024 + tt * 128: et * 1024 + (tt + 1) * 128]
                        rhs = WQs[slot][:, et * 512:(et + 1) * 512]
                        p.emit("tensor", lambda e, pb=pb, lhsT=lhsT, rhs=rhs, et=et:
                               e.matmul(pb, lhsT, rhs, start=(et == 0), stop=(et == 15)),
                               deps=[wl_tok, XQT, qkv_bank_tok[bk]], inc="qkv_mm")
                    mm_tok = p.tok("qkv_mm")
                    exv = CV[:, 120 + tt:121 + tt]
                    if oc == 0:
                        p.emit("vector", lambda e, exv=exv, tt=tt:
                               e.tensor_mul(exv, CV[:, 16 + tt:17 + tt], CV[:, 42:43]),
                               deps=[WSC, p.tok("xq_s")], inc="vexv")
                    tv = p.emit("vector", lambda e, pb=pb, exv=exv:
                                e.tensor_scalar(Vst[:, 0:512], pb, exv, None, ALU.mult),
                                deps=[mm_tok, p.tok("vexv"),
                                      ("vst_st", p.count.get("vst_st", 0))],
                                inc="qkv_ev")
                    qkv_bank_tok[bk] = tv
                    p.dma("sync", v_loc[tt * 128:(tt + 1) * 128, oc * 512:(oc + 1) * 512],
                          Vst[:, 0:512], deps=[tv], inc="vst_st")
            slot_free[slot] = p.tok("qkv_ev")
    QT_DONE = p.tok("qkv_ev")
    KV_ST = [p.tok("kst_st"), p.tok("vst_st")]

    t_k = p.emit("gpsimd", lambda e: e.collective_compute(
        "AllGather", ALU.bypass, replica_groups=[list(range(CORES))],
        ins=[kt_loc.opt()], outs=[kt_g.opt()]),
        deps=KV_ST + [QT_DONE], inc="cc_kt")
    t_v = p.emit("gpsimd", lambda e: e.collective_compute(
        "AllGather", ALU.bypass, replica_groups=[list(range(CORES))],
        ins=[v_loc.opt()], outs=[v_g.opt()]),
        deps=[t_k], inc="cc_vv")
    CCKV = [t_k, t_v]

    if debug == 3:
        d1 = dbgout("d_qt", [128, 16384], BF16)
        p.dma("sync", d1, Qb, deps=[QT_DONE], inc="dbg")
        d1b = dbgout("d_qt2", [128, 16384], BF16)
        p.dma("sync", d1b, Qb, deps=[QT_DONE] + CCKV + [("dbg", p.count.get("dbg", 0))], inc="dbg")
        d2 = dbgout("d_ktg", [4096, CH], BF16)
        p.dma("sync", d2, kt_g[0:4096, :], deps=CCKV, inc="dbg")
        d3 = dbgout("d_vg", [T, E], BF16)
        p.dma("sync", d3, v_g[0:T, :], deps=CCKV, inc="dbg")
        d4 = dbgout("d_tq1", [E, 512], BF16)
        p.dma("sync", d4, tern_out["wq"][:, 512:1024], deps=[QT_DONE], inc="dbg")
        return p.build([p.tok("dbg")]), dbg_list

    import os as _os
    _nh_qn = int(_os.environ.get("QN_HEADS", str(NH)))
    sqt = Bb[:, 0:1024]
    p.emit("vector", lambda e: e.memset(CV[0:1, 192:256], 0.0), deps=[QT_DONE], inc="qn_m")
    for h in range(_nh_qn):
        src = Qb[:, h * 1024:(h + 1) * 1024]
        p.emit("scalar", lambda e, src=src: e.activation(sqt, src, AF.Square),
               deps=[QT_DONE, ("qn_mm", p.count.get("qn_mm", 0))], inc="qn_sq")
        for hh in range(2):
            pb = bank(6)[0:1, :]
            p.emit("tensor", lambda e, hh=hh, pb=pb: e.matmul(
                pb, onescb, sqt[:, hh * 512:(hh + 1) * 512], start=True, stop=True),
                deps=[p.tok("qn_sq"), CST, ("qn_m", p.count.get("qn_m", 0))],
                inc="qn_mm")
            p.emit("vector", lambda e, h=h, hh=hh, pb=pb: e.reduce_max(
                CV[0:1, 192 + h:193 + h] if hh == 0 else CV[0:1, 224:225], pb, axis=AX.X),
                deps=[p.tok("qn_mm")], inc="qn_m")
        p.emit("vector", lambda e, h=h: e.tensor_max(CV[0:1, 192 + h:193 + h],
                                                     CV[0:1, 192 + h:193 + h], CV[0:1, 224:225]),
               deps=[p.tok("qn_m")], inc="qn_m")
    QN = p.tok("qn_m")
    if debug == 31:
        d1 = dbgout("d_qn", [1, 64], F32)
        p.dma("sync", d1, CV[0:1, 192:256], deps=[QN], inc="dbg")
        return p.build([p.tok("dbg")]), dbg_list
    knsc = Bb[:, 1024:2048]
    p.emit("vector", lambda e: e.memset(CV[0:1, 208:224], 0.0), deps=[QT_DONE], inc="kn_m")
    for hf in range(2):
        for h in range(NH):
            sl = Bb[:, 2048:2560]
            p.dma("sync", sl, kt_loc[hf * 2048 + h * 128: hf * 2048 + (h + 1) * 128, :],
                  deps=[p.tok("kst_st"), ("kn_m", max(0, p.count.get("kn_m", 0) - 1))], inc="kn_ld")
            p.emit("scalar", lambda e, sl=sl: e.activation(knsc[:, 0:512], sl, AF.Square),
                   deps=[p.tok("kn_ld"), ("kn_mm", p.count.get("kn_mm", 0))], inc="kn_sq")
            pb = bank(6)[0:1, :]
            p.emit("tensor", lambda e, pb=pb: e.matmul(pb, onescb, knsc[:, 0:512],
                                                       start=True, stop=True),
                   deps=[p.tok("kn_sq"), CST, ("kn_m", p.count.get("kn_m", 0))], inc="kn_mm")
            p.emit("vector", lambda e, pb=pb: e.reduce_max(CV[0:1, 224:225], pb, axis=AX.X),
                   deps=[p.tok("kn_mm")], inc="kn_m")
            p.emit("vector", lambda e, h=h: e.tensor_max(CV[0:1, 208 + h:209 + h],
                                                         CV[0:1, 208 + h:209 + h], CV[0:1, 224:225]),
                   deps=[p.tok("kn_m")], inc="kn_m")
    if debug == 32:
        d1 = dbgout("d_kn", [1, 64], F32)
        p.dma("sync", d1, CV[0:1, 192:256], deps=[p.tok("kn_m")], inc="dbg")
        return p.build([p.tok("dbg")]), dbg_list
    p.dma("sync", kn_in, CV[0:1, 208:224], deps=[p.tok("kn_m")], inc="kn_st")
    p.emit("gpsimd", lambda e: e.collective_compute(
        "AllGather", ALU.bypass, replica_groups=[list(range(CORES))],
        ins=[kn_in.opt()], outs=[kn_out.opt()]),
        deps=[p.tok("kn_st")], inc="cc_kn")
    knf = CV[0:1, 225:240]
    knall = Bf[0:1, 0:128]
    p.dma("sync", knall, kn_out.rearrange("r h -> (r h)").rearrange("(r h) -> h r", h=NH)
          if False else kn_out, deps=[p.tok("cc_kn")], inc="kn_ld2")
    p.emit("vector", lambda e: e.reduce_max(CV[0:1, 208:224],
                                            knall.rearrange("p (r h) -> p h r", r=CORES),
                                            axis=AX.X),
           deps=[p.tok("kn_ld2")], inc="kn_g")
    if debug == 33:
        d1 = dbgout("d_kng", [1, 64], F32)
        p.dma("sync", d1, CV[0:1, 192:256], deps=[p.tok("kn_g")], inc="dbg")
        return p.build([p.tok("dbg")]), dbg_list
    p.emit("vector", lambda e: e.tensor_mul(CV[0:1, 240:256], CV[0:1, 192:208], CV[0:1, 208:224]),
           deps=[p.tok("kn_g"), QN], inc="cb")
    p.emit("scalar", lambda e: e.activation(CV[0:1, 240:256], CV[0:1, 240:256], AF.Sqrt),
           deps=[p.tok("cb")], inc="cb")
    p.emit("vector", lambda e: e.tensor_scalar_mul(CV[0:1, 240:256], CV[0:1, 240:256], -1.0),
           deps=[p.tok("cb")], inc="cb")
    p.emit("tensor", lambda e: e.matmul(bank(6)[:, 0:16], onesr, CV[0:1, 240:256],
                                        start=True, stop=True),
           deps=[p.tok("cb"), CST], inc="cb_mm")
    negc = CV[:, 128:144]
    p.emit("vector", lambda e: e.tensor_copy(negc, bank(6)[:, 0:16]),
           deps=[p.tok("cb_mm")], inc="cb2")
    mrow = Bf[0:1, 128:152]
    p.dma("sync", mrow, mpat_in, deps=[p.tok("cb2")], inc="mp_ld")
    p.emit("tensor", lambda e: e.matmul(bank(6)[:, 0:NIT], onesr, mrow, start=True, stop=True),
           deps=[p.tok("mp_ld"), CST], inc="mp_mm")
    M24 = CV[:, 144:168]
    p.emit("vector", lambda e: e.tensor_copy(M24, bank(6)[:, 0:NIT]), deps=[p.tok("mp_mm")], inc="mp")
    BIGN = CV[:, 168:192]
    p.emit("vector", lambda e: e.tensor_scalar(BIGN, M24, -1.0, 1e30, ALU.add, ALU.mult),
           deps=[p.tok("mp")], inc="mp")
    for h in range(NH):
        p.emit("vector", lambda e, h=h: e.scalar_tensor_tensor(
            CB[:, h * 24:(h + 1) * 24], M24, negc[:, h:h + 1], BIGN, ALU.mult, ALU.add),
            deps=[p.tok("mp"), p.tok("cb2")], inc="cbias")
    CBIAS = p.tok("cbias")

    if debug == 35:
        d1 = dbgout("d_cbias", [128, 512], F32)
        p.dma("sync", d1, CB, deps=[CBIAS], inc="dbg")
        d2 = dbgout("d_norms", [1, 64], F32)
        p.dma("sync", d2, CV[0:1, 192:256], deps=[CBIAS, p.tok("sv")], inc="dbg")
        return p.build([p.tok("dbg")]), dbg_list

    idxk_sb = Ii32[:, 0:NIT]
    idxv_sb = Ii32[:, NIT:NIT * 5]
    p.dma("sync", idxk_sb, idxk_in, deps=[CBIAS], inc="idxl")
    p.dma("sync", idxv_sb, idxv_in, deps=[CBIAS], inc="idxl")
    IDX = p.tok("idxl")

    PT = [Ib[:, 1024 + i * 1024: 1024 + (i + 1) * 1024] for i in range(4)]
    KBUF = [Ib[:, 5120 + i * 512: 5120 + (i + 1) * 512] for i in range(2)]
    VBUF = [Ib[:, 6144 + i * 528: 6144 + (i + 1) * 528] for i in range(2)]
    for sl in range(2):
        for kt in range(4):
            p.emit("vector", lambda e, sl=sl, kt=kt: e.memset(VBUF[sl][:, kt * 132 + 128: kt * 132 + 129], 1.0),
                   deps=[XQT], inc="vones")
    VONES = p.tok("vones")

    kflat = kt_g.rearrange("a b -> (a b)")
    vflat = v_g.rearrange("a b -> (a b)")

    attn_out_region = Bf
    kslot_free = [("none", 0), ("none", 0)]
    vslot_free = [("none", 0), ("none", 0)]
    psS_tok = [("none", 0), ("none", 0)]
    psAV_tok = [("none", 0)] * 4

    import os
    _nh_att = int(os.environ.get("ATT_HEADS", str(NH)))
    _nit_cap = int(os.environ.get("ATT_ITERS", "99"))
    for h in range(_nh_att):
        for run, (qoff, nit) in enumerate(((0, min(ITER_A, _nit_cap)), (512, min(ITER_B, _nit_cap)))):
            it0 = 0 if run == 0 else ITER_A
            for it in range(nit):
                git = it0 + it
                ks = git % 2
                p.emit("gpsimd", lambda e, ks=ks, git=git, h=h: e.indirect_dma_start(
                    out=KBUF[ks], out_offset=None,
                    in_=kt_g, in_offset=bass.IndirectOffsetOnAxis(idxk_sb[:, git:git + 1], 0),
                    element_offset=h * 128 * CH),
                    deps=[IDX] + CCKV + [kslot_free[ks]], inc=f"kg{ks}", amt=16)
                kg_tok = p.tok(f"kg{ks}")
                vs = git % 2
                for kt in range(4):
                    p.emit("gpsimd", lambda e, vs=vs, kt=kt, git=git, h=h: e.indirect_dma_start(
                        out=VBUF[vs][:, kt * 132: kt * 132 + 128], out_offset=None,
                        in_=v_g, in_offset=bass.IndirectOffsetOnAxis(
                            idxv_sb[:, git * 4 + kt: git * 4 + kt + 1], 0),
                        element_offset=h * 128),
                        deps=[IDX] + CCKV + [VONES, vslot_free[vs]], inc=f"vg{vs}", amt=16)
                vg_tok = p.tok(f"vg{vs}")
                for half in range(2):
                    hb = half
                    pS = PS[:, hb * 1024: hb * 1024 + 1024]
                    for ktl in range(2):
                        kt = half * 2 + ktl
                        lhsT = KBUF[ks][:, kt * 128:(kt + 1) * 128]
                        rhs = Qb[:, h * 1024 + qoff: h * 1024 + qoff + 512]
                        pd = pS[:, ktl * 512:(ktl + 1) * 512]
                        p.emit("tensor", lambda e, pd=pd, lhsT=lhsT, rhs=rhs:
                               e.matmul(pd, lhsT, rhs, start=True, stop=True),
                               deps=[kg_tok, QT_DONE, psS_tok[hb]], inc="s_mm")
                    smm_tok = p.tok("s_mm")
                    pt = PT[hb * 2 + (git % 2)]
                    p.emit("scalar", lambda e, pS=pS, pt=pt, h=h, git=git:
                           e.activation(pt, pS, AF.Exp, bias=CB[:, h * 24 + git: h * 24 + git + 1]),
                           deps=[smm_tok, CBIAS,
                                 ("av_mm", max(0, p.count.get("av_mm", 0) - 8))], inc="expp")
                    exp_tok = p.tok("expp")
                    psS_tok[hb] = exp_tok
                    if it == 0:
                        p.emit("vector", lambda e, pt=pt, half=half: e.tensor_mul(
                            pt, pt, maskc[:, half * 1024:(half + 1) * 1024]),
                            deps=[exp_tok, CST], inc="maskm")
                        exp_tok = p.tok("maskm")
                    for ktl in range(2):
                        kt = half * 2 + ktl
                        for qt in range(4):
                            pav = PS[:, (4 + qt) * 512: (4 + qt) * 512 + 129]
                            lhsT = pt[:, ktl * 512 + qt * 128: ktl * 512 + (qt + 1) * 128]
                            rhs = VBUF[vs][:, kt * 132: kt * 132 + 129]
                            first = (it == 0 and kt == 0)
                            last = (it == nit - 1 and kt == 3)
                            p.emit("tensor", lambda e, pav=pav, lhsT=lhsT, rhs=rhs, first=first, last=last:
                                   e.matmul(pav, lhsT, rhs, start=first, stop=last),
                                   deps=[exp_tok, vg_tok] + ([psAV_tok[qt]] if first else []),
                                   inc="av_mm")
                kslot_free[ks] = p.tok("s_mm")
                vslot_free[vs] = p.tok("av_mm")
            av_done = p.tok("av_mm")
            for qt in range(4):
                pav = PS[:, (4 + qt) * 512: (4 + qt) * 512 + 129]
                tt = (qoff // 512) * 4 + qt
                rec = CV[:, 224:225]
                p.emit("vector", lambda e, pav=pav, rec=rec: e.reciprocal(rec, pav[:, 128:129]),
                       deps=[av_done, ("ao_ev", max(0, p.count.get("ao_ev", 0)))], inc="ao_rc")
                tv = p.emit("vector", lambda e, pav=pav, rec=rec, tt=tt, h=h: e.tensor_scalar(
                    attn_out_region[:, tt * 2048 + h * 128: tt * 2048 + (h + 1) * 128],
                    pav[:, 0:128], rec, None, ALU.mult),
                    deps=[p.tok("ao_rc")], inc="ao_ev")
                psAV_tok[qt] = tv
    ATTN_DONE = p.tok("ao_ev")

    if debug == 4:
        d1 = dbgout("d_attn", [128, 16384], F32)
        p.dma("sync", d1, Bf, deps=[ATTN_DONE], inc="dbg")
        return p.build([p.tok("dbg")]), dbg_list

    for tt in range(NTT):
        at = Bf[:, tt * E:(tt + 1) * E]
        s127 = CV[:, 56 + tt:57 + tt]
        exo = CV[:, 48 + tt:49 + tt]
        mx = CV[:, 224:225]
        p.emit("vector", lambda e, at=at, mx=mx: e.reduce_max(mx, at, axis=AX.X, apply_absolute_value=True),
               deps=[ATTN_DONE, p.tok("oq_s")], inc="oq_s")
        p.emit("vector", lambda e, mx=mx: e.tensor_scalar_max(mx, mx, 1e-5), deps=[p.tok("oq_s")], inc="oq_s")
        p.emit("vector", lambda e, mx=mx, exo=exo: e.tensor_scalar_mul(exo, mx, 1.0 / 127.0),
               deps=[p.tok("oq_s")], inc="oq_s")
        p.emit("vector", lambda e, mx=mx, s127=s127: e.reciprocal(s127, mx), deps=[p.tok("oq_s")], inc="oq_s")
        p.emit("vector", lambda e, s127=s127: e.tensor_scalar_mul(s127, s127, 127.0),
               deps=[p.tok("oq_s")], inc="oq_s")
        scr = Ii32[:, 2048 + (tt % 2) * 2048: 2048 + (tt % 2 + 1) * 2048]
        p.emit("vector", lambda e, at=at, scr=scr, s127=s127: e.tensor_scalar(scr, at, s127, None, ALU.mult),
               deps=[p.tok("oq_s"), ("oq_bf", max(0, p.count.get("oq_bf", 0) - 1)),
                     ("ao_ev", p.count.get("ao_ev", 0))], inc="oq_i")
        aqb = Qb[:, tt * E:(tt + 1) * E]
        p.emit("scalar", lambda e, scr=scr, aqb=aqb: e.activation(aqb, scr, AF.Copy),
               deps=[p.tok("oq_i")], inc="oq_bf")
    for et in range(16):
        for tt in range(NTT):
            src = Qb[:, tt * E + et * 128: tt * E + (et + 1) * 128]
            pb = PSB[:, (6 + (p.count.get("aoT_t", 0) % 2)) * 1024:][:, 0:128]
            dst = Ib[:, et * 1024 + tt * 128: et * 1024 + (tt + 1) * 128]
            p.emit("tensor", lambda e, src=src, pb=pb: e.transpose(pb, src, idb),
                   deps=[("oq_bf", NTT), CST,
                         ("aoT_e", max(0, p.count.get("aoT_e", 0) - 1))], inc="aoT_t")
            p.emit("vector", lambda e, pb=pb, dst=dst: e.tensor_copy(dst, pb),
                   deps=[p.tok("aoT_t")], inc="aoT_e")
    AOT = p.tok("aoT_e")

    WOs = [Qb[:, i * 4096:(i + 1) * 4096] for i in range(2)]
    o_bank_tok = {i: ("none", 0) for i in range(4)}
    oslot_free = [("none", 0), ("none", 0)]
    for oc8 in range(8):
        slot = oc8 % 2
        src = tern_out["wo"].rearrange("(a p) c -> p a c", p=128)[:, :, oc8 * 256:(oc8 + 1) * 256]
        _d = WOs[slot].rearrange("p (a c) -> p a c", a=16)
        for q4 in range(4):
            p.dma("sync", _d[:, q4 * 4:(q4 + 1) * 4], src[:, q4 * 4:(q4 + 1) * 4],
                  deps=CCAT + [AOT, oslot_free[slot]], inc=f"qwl{slot}")
        wl_tok = p.tok(f"qwl{slot}")
        for tt in range(NTT):
            bk = (oc8 * 8 + tt) % 4
            pb = bank(bk)[:, 0:256]
            for et in range(16):
                lhsT = Ib[:, et * 1024 + tt * 128: et * 1024 + (tt + 1) * 128]
                rhs = WOs[slot][:, et * 256:(et + 1) * 256]
                p.emit("tensor", lambda e, pb=pb, lhsT=lhsT, rhs=rhs, et=et:
                       e.matmul(pb, lhsT, rhs, start=(et == 0), stop=(et == 15)),
                       deps=[wl_tok, AOT, o_bank_tok[bk]], inc="o_mm")
            mm_tok = p.tok("o_mm")
            sov = CV[:, 120 + tt:121 + tt]
            if oc8 == 0:
                p.emit("vector", lambda e, sov=sov, tt=tt: e.tensor_mul(
                    sov, CV[:, 48 + tt:49 + tt], CV[:, 43:44]),
                    deps=[WSC, p.tok("oq_s"), QT_DONE], inc="sov")
            tv = p.emit("vector", lambda e, pb=pb, sov=sov, tt=tt, oc8=oc8: e.scalar_tensor_tensor(
                Bf[:, tt * 2048 + oc8 * 256: tt * 2048 + (oc8 + 1) * 256], pb, sov,
                X[:, tt * 2048 + oc8 * 256: tt * 2048 + (oc8 + 1) * 256], ALU.mult, ALU.add),
                deps=[mm_tok, p.tok("sov"), ("aoT_e", p.count["aoT_e"])], inc="o_ev")
            o_bank_tok[bk] = tv
        oslot_free[slot] = p.tok("o_ev")
    Z1 = p.tok("o_ev")

    def layer_norm(src_region, dst_region, tag, deps_in):
        for tt in range(NTT):
            z = src_region[:, tt * E:(tt + 1) * E]
            bst = CV[:, 96:96 + 24]
            for blk in range(4):
                p.emit("vector", lambda e, z=z, bst=bst, blk=blk: e.bn_stats(
                    bst[:, blk * 6:(blk + 1) * 6], z[:, blk * 512:(blk + 1) * 512]),
                    deps=deps_in + [p.tok(tag)], inc=tag)
            mv = CV[:, 120:122]
            p.emit("vector", lambda e, bst=bst, mv=mv: e.bn_aggr(mv, bst), deps=[p.tok(tag)], inc=tag)
            vv = CV[:, 122:123]
            p.emit("vector", lambda e, mv=mv, vv=vv: e.tensor_scalar_add(vv, mv[:, 1:2], 1e-5),
                   deps=[p.tok(tag)], inc=tag)
            sq = CV[:, 123:124]
            p.emit("scalar", lambda e, vv=vv, sq=sq: e.activation(sq, vv, AF.Sqrt),
                   deps=[p.tok(tag)], inc=tag)
            r0 = CV[:, 124:125]
            p.emit("vector", lambda e, sq=sq, r0=r0: e.reciprocal(r0, sq), deps=[p.tok(tag)], inc=tag)
            tmp = CV[:, 125:126]
            p.emit("vector", lambda e, r0=r0, tmp=tmp: e.tensor_mul(tmp, r0, r0), deps=[p.tok(tag)], inc=tag)
            p.emit("vector", lambda e, vv=vv, tmp=tmp: e.tensor_mul(tmp, tmp, vv), deps=[p.tok(tag)], inc=tag)
            p.emit("vector", lambda e, tmp=tmp: e.tensor_scalar(tmp, tmp, -0.5, 1.5, ALU.mult, ALU.add),
                   deps=[p.tok(tag)], inc=tag)
            p.emit("vector", lambda e, r0=r0, tmp=tmp: e.tensor_mul(r0, r0, tmp), deps=[p.tok(tag)], inc=tag)
            nb = CV[:, 126:127]
            p.emit("vector", lambda e, mv=mv, r0=r0, nb=nb: e.tensor_mul(nb, mv[:, 0:1], r0),
                   deps=[p.tok(tag)], inc=tag)
            p.emit("vector", lambda e, nb=nb: e.tensor_scalar_mul(nb, nb, -1.0), deps=[p.tok(tag)], inc=tag)
            dst = dst_region[:, tt * E:(tt + 1) * E]
            p.emit("vector", lambda e, z=z, dst=dst, r0=r0, nb=nb:
                   e.tensor_scalar(dst, z, r0, nb, ALU.mult, ALU.add),
                   deps=[p.tok(tag)], inc=tag + "_ap")
        return p.tok(tag + "_ap")

    X1 = layer_norm(Bf, X, "ln1", [Z1])

    if debug == 5:
        d1 = dbgout("d_x1", [128, 16384], F32)
        p.dma("sync", d1, X, deps=[X1], inc="dbg")
        return p.build([p.tok("dbg")]), dbg_list

    xst = Qf[:, 0:1024]
    hst = Qf[:, 1024:2048]
    for A in range(16):
        for tt in range(NTT):
            src = X[:, tt * E + A * 128: tt * E + (A + 1) * 128]
            pb = bank(6)[:, 0:128]
            p.emit("tensor", lambda e, src=src, pb=pb: e.transpose(pb, src, idf),
                   deps=[X1, CST, ("f2_xe", p.count.get("f2_xe", 0))], inc="f2_xt")
            p.emit("vector", lambda e, pb=pb, tt=tt: e.tensor_copy(xst[:, tt * 128:(tt + 1) * 128], pb),
                   deps=[p.tok("f2_xt")], inc="f2_xe")
        xe_tok = p.tok("f2_xe")
        for tc in range(2):
            pb = bank(4 + tc)
            p.emit("tensor", lambda e, pb=pb, tc=tc: e.matmul(
                pb, h128, xst[:, tc * 512:(tc + 1) * 512], start=True, stop=True),
                deps=[xe_tok, CST, ("f2_he", max(0, p.count.get("f2_he", 0) - 1))], inc="f2_hm")
            p.emit("scalar", lambda e, pb=pb, tc=tc: e.activation(
                hst[:, tc * 512:(tc + 1) * 512], pb, AF.Copy, scale=float(1.0 / np.sqrt(E))),
                deps=[p.tok("f2_hm"), ("f2_st", p.count.get("f2_st", 0))], inc="f2_he")
        p.dma("sync", r1a[A * 128:(A + 1) * 128, :], hst,
              deps=[p.tok("f2_he")], inc="f2_st")
    F2ST = p.tok("f2_st")

    pkt = [Qf[:, 0:1024], Qf[:, 1024:2048]]
    Mx2 = Qf[:, 2048:3072]
    Mn2 = Qf[:, 3072:4096]
    r1a_4d = r1a.rearrange("(a q s) t -> a q s t", a=16, q=16, s=8)

    def r1a_pack(pk):
        return r1a_4d[:, pk].transpose([1, 0, 2])
    for pk in range(16):
        sl = pk % 2
        p.dma("sync", pkt[sl], r1a_pack(pk),
              deps=[F2ST, ("f2_be", max(0, p.count.get("f2_be", 0) - 1))], inc=f"qwl{sl}")
        pl_tok = p.tok(f"qwl{sl}")
        for tc in range(2):
            pb = bank(4 + tc)
            p.emit("tensor", lambda e, pb=pb, sl=sl, tc=tc: e.matmul(
                pb, bd16, pkt[sl][:, tc * 512:(tc + 1) * 512], start=True, stop=True),
                deps=[pl_tok, CST, ("f2_be", max(0, p.count.get("f2_be", 0) - 1))], inc="f2_bm")
            dst = Bf[:, pk * 1024 + tc * 512: pk * 1024 + (tc + 1) * 512]
            p.emit("vector", lambda e, pb=pb, dst=dst: e.tensor_copy(dst, pb),
                   deps=[p.tok("f2_bm")], inc="f2_be")
        be_tok = p.tok("f2_be")
        val = Bf[:, pk * 1024:(pk + 1) * 1024]
        if pk == 0:
            p.emit("vector", lambda e, val=val: e.tensor_copy(Mx2, val), deps=[be_tok], inc="f2_mx")
            p.emit("vector", lambda e, val=val: e.tensor_copy(Mn2, val), deps=[be_tok], inc="f2_mx")
        else:
            p.emit("vector", lambda e, val=val: e.tensor_tensor(Mx2, Mx2, val, ALU.max),
                   deps=[be_tok, p.tok("f2_mx")], inc="f2_mx")
            p.emit("vector", lambda e, val=val: e.tensor_tensor(Mn2, Mn2, val, ALU.min),
                   deps=[be_tok, p.tok("f2_mx")], inc="f2_mx")
    p.emit("vector", lambda e: e.tensor_scalar_mul(Mn2, Mn2, -1.0), deps=[p.tok("f2_mx")], inc="f2_mx")
    p.emit("vector", lambda e: e.tensor_tensor(Mx2, Mx2, Mn2, ALU.max), deps=[p.tok("f2_mx")], inc="f2_mx")

    def quant_scales(Mx, srow, tag):
        TRS = Mn2 if Mx is Mx2 else Mn8
        p.emit("vector", lambda e: e.tensor_copy(TRS[0:64, :], Mx[64:128, :]),
               deps=[p.tok(tag)], inc=tag)
        p.emit("vector", lambda e: e.tensor_tensor(Mx[0:64, :], Mx[0:64, :], TRS[0:64, :], ALU.max),
               deps=[p.tok(tag)], inc=tag)
        p.emit("vector", lambda e: e.tensor_copy(TRS[0:32, :], Mx[32:64, :]),
               deps=[p.tok(tag)], inc=tag)
        p.emit("vector", lambda e: e.tensor_tensor(Mx[0:32, :], Mx[0:32, :], TRS[0:32, :], ALU.max),
               deps=[p.tok(tag)], inc=tag)
        p.emit("vector", lambda e: e.transpose(TRS[0:32, :], Mx[0:32, :]), deps=[p.tok(tag)], inc=tag)
        R32 = CV[0:32, 96:128]
        p.emit("vector", lambda e: e.reduce_max(
            R32, TRS[0:32, :].rearrange("p (b j) -> p b j", j=32), axis=AX.X),
            deps=[p.tok(tag)], inc=tag)
        p.emit("vector", lambda e: e.tensor_scalar_max(R32, R32, 1e-5), deps=[p.tok(tag)], inc=tag)
        E32 = CV[0:32, 128:160]
        p.emit("vector", lambda e: e.tensor_scalar_mul(E32, R32, 1.0 / 127.0), deps=[p.tok(tag)], inc=tag)
        S32 = CV[0:32, 160:192]
        p.emit("vector", lambda e: e.reciprocal(S32, R32), deps=[p.tok(tag)], inc=tag)
        p.emit("vector", lambda e: e.tensor_scalar_mul(S32, S32, 127.0), deps=[p.tok(tag)], inc=tag)
        p.emit("vector", lambda e: e.transpose(CV[0:32, 192:224], S32), deps=[p.tok(tag)], inc=tag)
        p.emit("vector", lambda e: e.transpose(CV[0:32, 224:256], E32), deps=[p.tok(tag)], inc=tag)
        p.dma("sync", srow[0:1, :].rearrange("a (b c) -> (a b) c", b=32), CV[0:32, 192:224],
              deps=[p.tok(tag)], inc=tag + "_st")
        p.dma("sync", srow[1:2, :].rearrange("a (b c) -> (a b) c", b=32), CV[0:32, 224:256],
              deps=[p.tok(tag)], inc=tag + "_st")
        return p.tok(tag + "_st")

    Mx8 = Bf[:, 0:1024]
    Mn8 = Bf[:, 1024:2048]
    SR2 = quant_scales(Mx2, srow2, "f2_mx")
    SM127a = Qf[:, 4096:5120]
    s2row = Qf[0:1, 5120:6144]
    p.dma("sync", s2row, srow2[0:1, :], deps=[SR2], inc="f2_sl")
    for hh in range(2):
        pb = bank(4 + hh)
        p.emit("tensor", lambda e, hh=hh, pb=pb: e.matmul(
            pb, onesr, s2row[0:1, hh * 512:(hh + 1) * 512], start=True, stop=True),
            deps=[p.tok("f2_sl"), CST, ("f2_sme", p.count.get("f2_sme", 0))], inc="f2_smm")
        p.emit("vector", lambda e, hh=hh, pb=pb: e.tensor_copy(SM127a[:, hh * 512:(hh + 1) * 512], pb),
               deps=[p.tok("f2_smm")], inc="f2_sme")
    SM127A = p.tok("f2_sme")
    qscr = Qi[:, 3072:4096]
    for pk in range(16):
        val = Bf[:, pk * 1024:(pk + 1) * 1024]
        p.emit("vector", lambda e, val=val: e.tensor_tensor(qscr, val, SM127a, ALU.mult),
               deps=[SM127A, p.tok("f2_cast")], inc="f2_q")
        dst = Ib[:, pk * 1024:(pk + 1) * 1024]
        p.emit("scalar", lambda e, dst=dst: e.activation(dst, qscr, AF.Copy),
               deps=[p.tok("f2_q")], inc="f2_cast")
    INT2048 = p.tok("f2_cast")

    if debug == 6:
        d1 = dbgout("d_int2048", [128, 16384], BF16)
        p.dma("sync", d1, Ib, deps=[INT2048], inc="dbg")
        d2 = dbgout("d_srow2", [2, T], F32)
        p.dma("sync", d2, srow2, deps=[SR2], inc="dbg")
        return p.build([p.tok("dbg")]), dbg_list

    SME = Qf[:, 0:1024]
    erow = Qf[0:1, 1024:2048]
    p.dma("sync", erow, srow2[1:2, :], deps=[SR2, INT2048], inc="gu_el")
    for hh in range(2):
        pb = bank(4 + hh)
        p.emit("tensor", lambda e, hh=hh, pb=pb: e.matmul(
            pb, onesr, erow[0:1, hh * 512:(hh + 1) * 512], start=True, stop=True),
            deps=[p.tok("gu_el"), CST, ("gu_sme", p.count.get("gu_sme", 0))], inc="gu_smm")
        p.emit("vector", lambda e, hh=hh, pb=pb: e.tensor_copy(SME[:, hh * 512:(hh + 1) * 512], pb),
               deps=[p.tok("gu_smm")], inc="gu_sme")
    SMED = p.tok("gu_sme")

    WG = [Bb[:, i * 8192:(i + 1) * 8192] for i in range(2)]
    WU = [Bb[:, 16384 + i * 8192: 16384 + (i + 1) * 8192] for i in range(2)]
    gtr = Qf[:, 2048:2560]
    gsl = Qf[:, 2560:3072]
    utr = Qf[:, 3072:3584]
    hti = Qf[:, 3584:4096]
    hs2 = Qf[:, 4096:5120]
    gslot_free = [("none", 0), ("none", 0)]
    uslot_free = [("none", 0), ("none", 0)]
    gu_bank = {i: ("none", 0) for i in range(6)}

    for oc in range(16):
        sl = oc % 2
        gsrc = tern_out["wg"].rearrange("(a p) c -> p a c", p=128)[:, :, oc * 512:(oc + 1) * 512]
        usrc = tern_out["wu"].rearrange("(a p) c -> p a c", p=128)[:, :, oc * 512:(oc + 1) * 512]
        _dg = WG[sl].rearrange("p (a c) -> p a c", a=16)
        _du = WU[sl].rearrange("p (a c) -> p a c", a=16)
        for q4 in range(4):
            p.dma("sync", _dg[:, q4 * 4:(q4 + 1) * 4], gsrc[:, q4 * 4:(q4 + 1) * 4],
                  deps=CCFFN + [INT2048, gslot_free[sl]], inc=f"qwl{sl}")
            p.dma("sync", _du[:, q4 * 4:(q4 + 1) * 4], usrc[:, q4 * 4:(q4 + 1) * 4],
                  deps=CCFFN + [INT2048, uslot_free[sl]], inc=f"qwl{2 + sl}")
        wl_tok_g = p.tok(f"qwl{sl}")
        wl_tok_u = p.tok(f"qwl{2 + sl}")
        for tc in range(2):
            for ot in range(4):
                A = oc * 4 + ot
                bkg = (tc * 4 + ot) % 2
                bku = 2 + (tc * 4 + ot) % 2
                bkh = 4 + (tc * 4 + ot) % 2
                pg = bank(bkg)
                pu = bank(bku)
                ph = bank(bkh)
                for et in range(16):
                    lg = WG[sl][:, et * 512 + ot * 128: et * 512 + (ot + 1) * 128]
                    rhs = Ib[:, et * 1024 + tc * 512: et * 1024 + (tc + 1) * 512]
                    p.emit("tensor", lambda e, pg=pg, lg=lg, rhs=rhs, et=et:
                           e.matmul(pg, lg, rhs, start=(et == 0), stop=(et == 15)),
                           deps=[wl_tok_g, INT2048, gu_bank[bkg]], inc="gu_mm")
                g_tok = p.tok("gu_mm")
                for et in range(16):
                    lu = WU[sl][:, et * 512 + ot * 128: et * 512 + (ot + 1) * 128]
                    rhs = Ib[:, et * 1024 + tc * 512: et * 1024 + (tc + 1) * 512]
                    p.emit("tensor", lambda e, pu=pu, lu=lu, rhs=rhs, et=et:
                           e.matmul(pu, lu, rhs, start=(et == 0), stop=(et == 15)),
                           deps=[wl_tok_u, INT2048, gu_bank[bku]], inc="gu_mm")
                u_tok = p.tok("gu_mm")
                tg = p.emit("vector", lambda e, pg=pg, tc=tc: e.scalar_tensor_tensor(
                    gtr, pg, CV[:, 44:45], SME[:, tc * 512:(tc + 1) * 512], ALU.mult, ALU.mult),
                    deps=[g_tok, WSC, SMED, p.tok("gu_sil")], inc="gu_gt")
                gu_bank[bkg] = tg
                p.emit("scalar", lambda e: e.activation(gsl, gtr, AF.Silu),
                       deps=[tg, p.tok("gu_h")], inc="gu_sil")
                tu = p.emit("vector", lambda e, pu=pu, tc=tc: e.scalar_tensor_tensor(
                    utr, pu, CV[:, 45:46], SME[:, tc * 512:(tc + 1) * 512], ALU.mult, ALU.mult),
                    deps=[u_tok, WSC, SMED, p.tok("gu_h")], inc="gu_ut")
                gu_bank[bku] = tu
                p.emit("gpsimd", lambda e: e.tensor_mul(hti, gsl, utr),
                       deps=[p.tok("gu_sil"), tu, p.tok("gu_hm")], inc="gu_h")
                h_tok = p.tok("gu_h")
                p.emit("tensor", lambda e, ph=ph: e.matmul(ph, h128, hti, start=True, stop=True),
                       deps=[h_tok, CST, gu_bank[bkh]], inc="gu_hm")
                hm_tok = p.tok("gu_hm")
                th = p.emit("scalar", lambda e, ph=ph: e.activation(
                    hs2[:, 0:512], ph, AF.Copy, scale=float(1.0 / np.sqrt(FF))),
                    deps=[hm_tok, ("gu_hst", p.count.get("gu_hst", 0))], inc="gu_he")
                gu_bank[bkh] = th
                p.dma("sync", r1b[A * 128:(A + 1) * 128, tc * 512:(tc + 1) * 512], hs2[:, 0:512],
                      deps=[th], inc="gu_hst")
        gslot_free[sl] = p.tok("gu_gt")
        uslot_free[sl] = p.tok("gu_ut")
    GU_DONE = p.tok("gu_hst")

    Mx8 = Bf[:, 2048:3072]
    Mn8 = Bf[:, 3072:4096]
    pk8 = [Bf[:, 0:1024], Bf[:, 1024:2048]]
    vst8 = [Bf[:, 4096:5120], Bf[:, 5120:6144]]
    r1b_4d = r1b.rearrange("(a q s) t -> a q s t", a=64, q=64, s=2)
    for pk in range(64):
        sl = pk % 2
        p.dma("sync", pk8[sl], r1b_4d[:, pk].transpose([1, 0, 2]),
              deps=[GU_DONE, ("f8_be", max(0, p.count.get("f8_be", 0) - 1))], inc=f"qwl{sl}")
        pl_tok = p.tok(f"qwl{sl}")
        for tc in range(2):
            pb = bank(4 + tc)
            p.emit("tensor", lambda e, pb=pb, sl=sl, tc=tc: e.matmul(
                pb, bd64, pk8[sl][:, tc * 512:(tc + 1) * 512], start=True, stop=True),
                deps=[pl_tok, CST, ("f8_be", max(0, p.count.get("f8_be", 0) - 1))], inc="f8_bm")
            dst = vst8[sl][:, tc * 512:(tc + 1) * 512]
            p.emit("vector", lambda e, pb=pb, dst=dst: e.tensor_copy(dst, pb),
                   deps=[p.tok("f8_bm"), p.tok(f"fst{sl}")], inc="f8_be")
        be_tok = p.tok("f8_be")
        val = vst8[sl]
        p.dma("sync", r2[pk * 128:(pk + 1) * 128, :], val, deps=[be_tok], inc=f"fst{sl}")
        if pk == 0:
            p.emit("vector", lambda e, val=val: e.tensor_copy(Mx8, val), deps=[be_tok], inc="f8_mx")
            p.emit("vector", lambda e, val=val: e.tensor_copy(Mn8, val), deps=[be_tok], inc="f8_mx")
        else:
            p.emit("vector", lambda e, val=val: e.tensor_tensor(Mx8, Mx8, val, ALU.max),
                   deps=[be_tok, p.tok("f8_mx")], inc="f8_mx")
            p.emit("vector", lambda e, val=val: e.tensor_tensor(Mn8, Mn8, val, ALU.min),
                   deps=[be_tok, p.tok("f8_mx")], inc="f8_mx")
    p.emit("vector", lambda e: e.tensor_scalar_mul(Mn8, Mn8, -1.0), deps=[p.tok("f8_mx")], inc="f8_mx")
    p.emit("vector", lambda e: e.tensor_tensor(Mx8, Mx8, Mn8, ALU.max), deps=[p.tok("f8_mx")], inc="f8_mx")
    SR8 = quant_scales(Mx8, srow8, "f8_mx")

    SM127b = Qf[:, 0:1024]
    s8row = Qf[0:1, 1024:2048]
    p.dma("sync", s8row, srow8[0:1, :], deps=[SR8], inc="f8_sl")
    for hh in range(2):
        pb = bank(4 + hh)
        p.emit("tensor", lambda e, hh=hh, pb=pb: e.matmul(
            pb, onesr, s8row[0:1, hh * 512:(hh + 1) * 512], start=True, stop=True),
            deps=[p.tok("f8_sl"), CST, ("f8_sme", p.count.get("f8_sme", 0))], inc="f8_smm")
        p.emit("vector", lambda e, hh=hh, pb=pb: e.tensor_copy(SM127b[:, hh * 512:(hh + 1) * 512], pb),
               deps=[p.tok("f8_smm")], inc="f8_sme")
    SM127B = p.tok("f8_sme")
    for tt in range(NTT):
        p.dma("sync", CV[:, 48 + tt:49 + tt],
              srow8[1:2, tt * 128:(tt + 1) * 128].rearrange("a b -> b a"),
              deps=[SR8], inc="f8_ev")
    for tt in range(NTT):
        p.emit("vector", lambda e, tt=tt: e.tensor_mul(CV[:, 56 + tt:57 + tt],
                                                       CV[:, 48 + tt:49 + tt], CV[:, 46:47]),
               deps=[("f8_ev", 16 * NTT), WSC], inc="f8_sdv")
    SDV = p.tok("f8_sdv")

    WD = [Qb[:, 8192 + i * 4096: 8192 + (i + 1) * 4096] for i in range(2)]
    qscr8 = Qi[:, 3072:4096]
    rp8 = [Qf[:, 2048:3072], Qf[:, 2048:3072]]
    dn_bank = {i: ("none", 0) for i in range(4)}
    wdslot_free = [("none", 0), ("none", 0)]
    for jq in range(4):
        for pl in range(16):
            pk = jq * 16 + pl
            p.dma("sync", rp8[0], r2[pk * 128:(pk + 1) * 128, :],
                  deps=[p.tok("fst0"), p.tok("fst1"), SM127B,
                        ("dn_cast", max(0, p.count.get("dn_cast", 0)))],
                  inc="dn_rl")
            p.emit("vector", lambda e: e.tensor_tensor(qscr8, rp8[0], SM127b, ALU.mult),
                   deps=[p.tok("dn_rl")], inc="dn_q")
            dst = Ib[:, pl * 1024:(pl + 1) * 1024]
            p.emit("scalar", lambda e, dst=dst: e.activation(dst, qscr8, AF.Copy),
                   deps=[p.tok("dn_q"), ("dn_mm", max(0, p.count.get("dn_mm", 0)))], inc="dn_cast")
        int_tok = p.tok("dn_cast")
        for oc8 in range(8):
            sl = oc8 % 2
            src = tern_out["wd"].rearrange("(a p) c -> p a c", p=128)[
                :, jq * 16:(jq + 1) * 16, oc8 * 256:(oc8 + 1) * 256]
            _dd = WD[sl].rearrange("p (a c) -> p a c", a=16)
            for q4 in range(4):
                p.dma("sync", _dd[:, q4 * 4:(q4 + 1) * 4], src[:, q4 * 4:(q4 + 1) * 4],
                      deps=CCFFN + [int_tok, wdslot_free[sl]], inc=f"qwl{sl}")
            wl_tok = p.tok(f"qwl{sl}")
            for tt in range(NTT):
                bk = (oc8 * 8 + tt) % 4
                pb = bank(bk)[:, 0:256]
                for pl in range(16):
                    lhsT = Ib[:, pl * 1024 + tt * 128: pl * 1024 + (tt + 1) * 128]
                    rhs = WD[sl][:, pl * 256:(pl + 1) * 256]
                    p.emit("tensor", lambda e, pb=pb, lhsT=lhsT, rhs=rhs, pl=pl:
                           e.matmul(pb, lhsT, rhs, start=(pl == 0), stop=(pl == 15)),
                           deps=[wl_tok, int_tok, dn_bank[bk]], inc="dn_mm")
                mm_tok = p.tok("dn_mm")
                zdst = Bf[:, tt * 2048 + oc8 * 256: tt * 2048 + (oc8 + 1) * 256]
                if jq == 0:
                    tv = p.emit("vector", lambda e, pb=pb, zdst=zdst: e.tensor_copy(zdst, pb),
                                deps=[mm_tok, SDV], inc="dn_z")
                else:
                    tv = p.emit("vector", lambda e, pb=pb, zdst=zdst: e.tensor_tensor(zdst, zdst, pb, ALU.add),
                                deps=[mm_tok, p.tok("dn_z")], inc="dn_z")
                dn_bank[bk] = tv
            wdslot_free[sl] = p.tok("dn_z")
    DN = p.tok("dn_z")
    for tt in range(NTT):
        zt = Bf[:, tt * E:(tt + 1) * E]
        p.emit("vector", lambda e, zt=zt, tt=tt: e.tensor_scalar(zt, zt, CV[:, 56 + tt:57 + tt],
                                                                 None, ALU.mult),
               deps=[DN, SDV], inc="dn_fin")
        p.emit("vector", lambda e, zt=zt, tt=tt: e.tensor_add(zt, zt, X[:, tt * E:(tt + 1) * E]),
               deps=[p.tok("dn_fin")], inc="dn_fin")
    ZFIN = p.tok("dn_fin")
    X2 = layer_norm(Bf, X, "ln2", [ZFIN])
    for tt in range(NTT):
        p.dma("sync", out[tt * 128:(tt + 1) * 128, :], X[:, tt * E:(tt + 1) * E],
              deps=[X2], inc="out_st")
    FINAL = [p.tok("out_st")]
    if debug:
        FINAL.append(p.tok("dbg"))

    nc = p.build(FINAL)
    return nc, dbg_list



def _host_inputs(inputs, core):
    x = np.asarray(inputs["L0_attn_x"]).reshape(S, E)
    cq0, cq1 = core, 15 - core
    xs = np.concatenate([x[cq0 * CH:(cq0 + 1) * CH], x[cq1 * CH:(cq1 + 1) * CH]], axis=0)

    p2 = _perm2048()
    p8 = _perm8192()
    wqT = np.ascontiguousarray(np.asarray(inputs["L0_attn_w_q"]).T)
    wkT = np.ascontiguousarray(np.asarray(inputs["L0_attn_w_k"]).T)
    wvT = np.ascontiguousarray(np.asarray(inputs["L0_attn_w_v"]).T)
    woT = np.ascontiguousarray(np.asarray(inputs["L0_attn_w_o"]).T)
    wgT = np.ascontiguousarray(np.asarray(inputs["L0_ffn_w_gate"]).T[p2, :])
    wuT = np.ascontiguousarray(np.asarray(inputs["L0_ffn_w_up"]).T[p2, :])
    wdT = np.ascontiguousarray(np.asarray(inputs["L0_ffn_w_down"]).T[p8, :])

    def sl(w, nparts):
        r = w.shape[0] // 8
        return np.ascontiguousarray(w[core * r:(core + 1) * r])

    h128 = _hadamard(128)
    h16 = _hadamard(16)
    h64 = _hadamard(64)
    bd16 = np.zeros((128, 128), np.float32)
    for s_ in range(8):
        bd16[s_ * 16:(s_ + 1) * 16, s_ * 16:(s_ + 1) * 16] = h16
    bd64 = np.zeros((128, 128), np.float32)
    for s_ in range(2):
        bd64[s_ * 64:(s_ + 1) * 64, s_ * 64:(s_ + 1) * 64] = h64

    mask = np.zeros((128, 2048), np.float32)
    qv = np.arange(CH)
    for kt in range(4):
        kvp = np.arange(128) + kt * 128
        mask[:, kt * CH:(kt + 1) * CH] = (qv[None, :] >= kvp[:, None]).astype(np.float32)

    idxk = np.zeros((128, NIT), np.int32)
    idxv = np.zeros((128, NIT * 4), np.int32)
    mpat = np.zeros((1, NIT), np.float32)
    pvec = np.arange(128)
    for run, (cq, base_it, nit) in enumerate(((cq0, 0, ITER_A), (cq1, ITER_A, ITER_B))):
        for it in range(nit):
            j = cq - it
            git = base_it + it
            if j >= 0:
                mpat[0, git] = 1.0
                r = j if j <= 7 else 15 - j
                hf = 0 if j <= 7 else 1
                idxk[:, git] = r * 4096 + hf * 2048 + pvec
                vbase = r * T + hf * CH
                for kt in range(4):
                    idxv[:, git * 4 + kt] = vbase + kt * 128 + pvec
    consts = {
        "x": xs.astype(np.float32),
        "wq_sl": sl(wqT, 8).astype(np.float32), "wk_sl": sl(wkT, 8).astype(np.float32),
        "wv_sl": sl(wvT, 8).astype(np.float32), "wo_sl": sl(woT, 8).astype(np.float32),
        "wg_sl": sl(wgT, 8).astype(np.float32), "wu_sl": sl(wuT, 8).astype(np.float32),
        "wd_sl": sl(wdT, 8).astype(np.float32),
        "h128": h128, "bd16": bd16, "bd64": bd64,
        "idf": np.eye(128, dtype=np.float32),
        "idb": np.eye(128, dtype=np.float32).astype(BF),
        "onesr": np.ones((1, 128), np.float32),
        "onescb": np.ones((128, 1), np.float32).astype(BF),
        "maskc": mask.astype(BF),
        "mpat": mpat, "idxk": idxk, "idxv": idxv,
    }
    return consts


_NC_CACHE = {}


def _get_nc(debug):
    if debug not in _NC_CACHE:
        _NC_CACHE[debug] = build_program(debug)
    return _NC_CACHE[debug]


def kernel(**inputs):
    nc, _ = _get_nc(0)
    in_maps = [_host_inputs(inputs, c) for c in range(CORES)]
    res = run_bass_kernel_spmd(nc, in_maps, core_ids=list(range(CORES)))
    full = np.empty((S, E), np.float32)
    for c in range(CORES):
        o = res.results[c]["out"]
        full[c * CH:(c + 1) * CH] = o[0:CH]
        full[(15 - c) * CH:(16 - c) * CH] = o[CH:]
    return full.reshape(1, S, E)
